# version 07
# baseline (speedup 1.0000x reference)
import math
import contextlib
import numpy as np
import ml_dtypes
import concourse.bass as bass
import concourse.mybir as mybir
from concourse.bass_utils import run_bass_kernel_spmd

F32 = mybir.dt.float32
BF16 = mybir.dt.bfloat16
F16 = mybir.dt.float16
I32 = mybir.dt.int32
AX = mybir.AxisListType.X
OP = mybir.AluOpType
AF = mybir.ActivationFunctionType

D = 4096
NCH = 32
NCORES = 8
TPC = 1024
PASS = 256
NPASS = TPC // PASS
NSUB = PASS // 128
EPS = 1e-5
NSLOT = 20
WINDOW, STRIPES = 6, 3
DEBUG = False
BARRIERS = set()


def _band_ok():
    stride = int(math.sqrt(D))
    nz = set()
    for k in range(STRIPES):
        ps = stride * k
        for i in range(0, D, WINDOW):
            r0, r1 = ps + i, min(ps + i + WINDOW, D)
            c0, c1 = i, min(i + WINDOW, D)
            if r0 >= D or c0 >= D or r1 <= r0 or c1 <= c0:
                continue
            for (a0, a1, b0, b1) in ((r0, r1, c0, c1), (c0, c1, r0, r1)):
                for I in range(a0 // 128, (a1 - 1) // 128 + 1):
                    for J in range(b0 // 128, (b1 - 1) // 128 + 1):
                        nz.add((I, J))
    assert all(abs(I - J) <= 2 for I, J in nz)
    return nz


_BAND = _band_ok()


def band_of(J):
    return [I for I in range(max(0, J - 2), min(NCH, J + 3))]


def tile_slot(I, J):
    return I // 4, (I % 4) * 5 + (J - I + 2)


class Sched:

    def __init__(self, emit_eng, eng_obj, sems):
        self.emit = emit_eng
        self.e = eng_obj
        self.sem = sems
        self.cnt = {k: 0 for k in sems}
        self.last_wait = {}

    def val(self, s):
        return self.cnt[s]

    def tok(self, s):
        return (s, self.cnt[s])

    def op(self, eng, fn, waits=(), inc=None, inc_by=1):
        if eng == self.emit:
            for s, v in waits:
                if v > 0 and self.last_wait.get(s, -1) < v:
                    self.e.wait_ge(self.sem[s], v)
                    self.last_wait[s] = v
            inst = fn(self.e)
            if inc is not None:
                inst.then_inc(self.sem[inc], inc_by)
        if inc is not None:
            self.cnt[inc] += inc_by

    def wait(self, eng, waits):
        if eng == self.emit:
            for s, v in waits:
                if v > 0 and self.last_wait.get(s, -1) < v:
                    self.e.wait_ge(self.sem[s], v)
                    self.last_wait[s] = v


def build_program():
    nc = bass.Bass("TRN2", target_bir_lowering=False, debug=False)

    xin = nc.dram_tensor("x", [TPC, D], F32, kind="ExternalInput").ap()
    w1s = nc.dram_tensor("w1s", [D, 512], F32, kind="ExternalInput").ap()
    w2s = nc.dram_tensor("w2s", [1024, D], F32, kind="ExternalInput").ap()
    masks = nc.dram_tensor("masks", [4, 5, 128, 128], F32, kind="ExternalInput").ap()
    b2t = nc.dram_tensor("b2t", [128, 32], F32, kind="ExternalInput").ap()
    packeds = nc.dram_tensor("packeds", [512, D // 2], I32, kind="ExternalInput").ap()
    scaless = nc.dram_tensor("scaless", [512, 128], F16, kind="ExternalInput").ap()
    eyef = nc.dram_tensor("eyef", [128, 128], F32, kind="ExternalInput").ap()
    eyeb = nc.dram_tensor("eyeb", [128, 128], BF16, kind="ExternalInput").ap()
    ic2 = nc.dram_tensor("ic2", [128, 2], I32, kind="ExternalInput").ap()
    epsb = nc.dram_tensor("epsb", [128, 1], F32, kind="ExternalInput").ap()
    yout = nc.dram_tensor("y", [TPC, D], F32, kind="ExternalOutput").ap()
    if DEBUG:
        dbg_x1 = nc.dram_tensor("dbg_x1", [TPC, D], F32, kind="ExternalOutput").ap()
        dbg_z = nc.dram_tensor("dbg_z", [TPC, D], F32, kind="ExternalOutput").ap()
        dbg_a = nc.dram_tensor("dbg_a", [8, 2, 128, NSLOT, 128], BF16, kind="ExternalOutput").ap()
        dbg_w = nc.dram_tensor("dbg_w", [2, 4, 128, NCH, 128], BF16, kind="ExternalOutput").ap()
        dbg_ht = nc.dram_tensor("dbg_ht", [2, 128, NCH * PASS], BF16, kind="ExternalOutput").ap()
        dbg_g = nc.dram_tensor("dbg_g", [PASS, D], F32, kind="ExternalOutput").ap()

    cca_in = nc.dram_tensor("cca_in", [2, 128, NSLOT, 128], BF16, kind="Internal").ap()
    cca_out = nc.dram_tensor("cca_out", [8, 2, 128, NSLOT, 128], BF16, kind="Internal").ap()
    ccw_in = nc.dram_tensor("ccw_in", [2, 4, 128, NCH, 128], BF16, kind="Internal").ap()
    ccw_out = nc.dram_tensor("ccw_out", [8, 2, 4, 128, NCH, 128], BF16, kind="Internal").ap()

    with contextlib.ExitStack() as stack:
        E = stack.enter_context

        eyef_s = E(nc.sbuf_tensor("eyef_s", [128, 128], F32))
        eyeb_s = E(nc.sbuf_tensor("eyeb_s", [128, 128], BF16))
        ic_s = E(nc.sbuf_tensor("ic_s", [128, 2], I32))
        b2_s = E(nc.sbuf_tensor("b2_s", [128, 32], F32))
        stat = E(nc.sbuf_tensor("stat", [128, 64], F32))
        sc2 = E(nc.sbuf_tensor("sc2", [128, 8], F32))
        eps_s = E(nc.sbuf_tensor("eps_s", [128, 1], F32))

        sems = {}
        for name in ("dc", "dw1", "dw2", "dpk", "dca", "dcw", "dares", "dx",
                     "dwh0", "dwh1", "dwl0", "dwl1", "dy",
                     "v", "ac", "pe", "cc", "dd"):
            sems[name] = E(nc.semaphore(name))

        stA = contextlib.ExitStack()
        EA = stA.enter_context
        pk = EA(nc.sbuf_tensor("pk", [128, 512], I32))
        nib = EA(nc.sbuf_tensor("nib", [128, 1024], I32))
        wf = EA(nc.sbuf_tensor("wf", [128, 1024], F32))
        scl = EA(nc.sbuf_tensor("scl", [128, 128], F16))
        esc = EA(nc.sbuf_tensor("esc", [128, 512], F32))
        w16 = EA(nc.sbuf_tensor("w16", [128, 1024], F16))
        wde_hi = EA(nc.sbuf_tensor("wde_hi", [128, 1024], BF16))
        wde_lo = EA(nc.sbuf_tensor("wde_lo", [128, 1024], BF16))
        wt_hi = EA(nc.sbuf_tensor("wt_hi", [128, NCH * 128], BF16))
        wt_lo = EA(nc.sbuf_tensor("wt_lo", [128, NCH * 128], BF16))
        w1stg = EA(nc.sbuf_tensor("w1stg", [128, D], F32))
        w1hi = EA(nc.sbuf_tensor("w1hi", [128, 4 * D], BF16))
        w1lo = EA(nc.sbuf_tensor("w1lo", [128, 4 * D], BF16))
        w2row = EA(nc.sbuf_tensor("w2row", [128, D], F32))
        w2hi = EA(nc.sbuf_tensor("w2hi", [128, D], BF16))
        w2lo = EA(nc.sbuf_tensor("w2lo", [128, D], BF16))
        w2t_hi = EA(nc.sbuf_tensor("w2t_hi", [128, D], BF16))
        w2t_lo = EA(nc.sbuf_tensor("w2t_lo", [128, D], BF16))
        mkall = EA(nc.sbuf_tensor("mkall", [128, NSLOT * 128], F32))
        amf = EA(nc.sbuf_tensor("amf", [128, 128], F32))
        a_hi = EA(nc.sbuf_tensor("a_hi", [128, NSLOT * 128], BF16))
        a_lo = EA(nc.sbuf_tensor("a_lo", [128, NSLOT * 128], BF16))

        pb = [E(nc.psum_tensor(f"pb{i}", [128, 512], F32)) for i in range(8)]
        pbb = [p[:].bitcast(BF16) for p in pb]

        stA.close()

        ares = E(nc.sbuf_tensor("ares", [128, 8 * NSLOT * 2 * 128], BF16))
        x0 = E(nc.sbuf_tensor("x0", [128, NSUB * D], F32))
        gz = E(nc.sbuf_tensor("gz", [128, NSUB * D], F32))
        hT_hi = E(nc.sbuf_tensor("hT_hi", [128, NCH * PASS], BF16))
        hT_lo = E(nc.sbuf_tensor("hT_lo", [128, NCH * PASS], BF16))
        wtb_hi = [E(nc.sbuf_tensor(f"wtb_hi{i}", [128, 16 * 128], BF16)) for i in range(2)]
        wtb_lo = [E(nc.sbuf_tensor(f"wtb_lo{i}", [128, 16 * 128], BF16)) for i in range(2)]
        hch = E(nc.sbuf_tensor("hch", [128, 2 * 128], F32))
        gts = E(nc.sbuf_tensor("gts", [128, 2 * PASS], F32))

        def program(S: Sched):
            for dst, srcap in ((eyef_s, eyef), (eyeb_s, eyeb), (ic_s, ic2),
                               (b2_s, b2t), (eps_s, epsb)):
                S.op("sync", lambda e, d=dst, s=srcap: e.dma_start(out=d[:], in_=s),
                     inc="dc", inc_by=16)
            S.op("sync", lambda e: e.dma_start(
                out=mkall[:].rearrange("p (il dj t) -> p il dj t", il=4, dj=5),
                in_=masks.transpose([2, 0, 1, 3])), inc="dc", inc_by=16)
            t_const = ("dc", 96)
            t_mk = ("dc", 96)
            S.wait("tensor", [t_const])

            s15 = ic_s[:, 0:1]
            s240 = ic_s[:, 1:2]

            w1v = w1s.rearrange("(dc p) s -> dc p s", p=128)
            for il in range(4):
                S.op("sync", lambda e, il=il: e.dma_start(
                    out=w1stg[:].rearrange("p (dc s) -> p dc s", dc=NCH),
                    in_=w1v[:, :, il * 128:(il + 1) * 128].transpose([1, 0, 2])),
                    waits=[("v", S.val("v"))], inc="dw1", inc_by=16)
                t_w1 = S.tok("dw1")
                S.op("vector", lambda e, il=il: e.tensor_copy(
                    w1hi[:, il * D:(il + 1) * D], w1stg[:]),
                    waits=[t_w1], inc="v")
                S.op("vector", lambda e, il=il: e.tensor_sub(
                    w1lo[:, il * D:(il + 1) * D], w1stg[:],
                    w1hi[:, il * D:(il + 1) * D]),
                    waits=[("v", S.val("v"))], inc="v")
            t_w1split = S.tok("v")

            for jl in range(8):
                S.op("sync", lambda e, jl=jl: e.dma_start(
                    out=w2row[:], in_=w2s[jl * 128:(jl + 1) * 128, :]),
                    waits=[("v", S.val("v"))], inc="dw2", inc_by=16)
                t_w2r = S.tok("dw2")
                S.op("vector", lambda e: e.tensor_copy(w2hi[:], w2row[:]),
                     waits=[t_w2r, ("pe", S.val("pe"))], inc="v")
                S.op("vector", lambda e: e.tensor_sub(w2lo[:], w2row[:], w2hi[:]),
                     waits=[("v", S.val("v"))], inc="v")
                t_w2split = S.tok("v")
                for dc in range(NCH):
                    pq_hi = pbb[4 + dc % 2]
                    pq_lo = pbb[6 + dc % 2]
                    S.op("tensor", lambda e, dc=dc, pq=pq_hi: e.transpose(
                        pq[:, 0:128], w2hi[:, dc * 128:(dc + 1) * 128], eyeb_s[:]),
                        waits=[t_w2split, ("ac", S.val("ac"))], inc="pe")
                    t_tr1 = S.tok("pe")
                    S.op("tensor", lambda e, dc=dc, pq=pq_lo: e.transpose(
                        pq[:, 0:128], w2lo[:, dc * 128:(dc + 1) * 128], eyeb_s[:]),
                        waits=[("v", S.val("v"))], inc="pe")
                    t_tr2 = S.tok("pe")
                    S.op("scalar", lambda e, dc=dc, pq=pq_hi: e.copy(
                        w2t_hi[:, dc * 128:(dc + 1) * 128], pq[:, 0:128]),
                        waits=[t_tr1], inc="ac")
                    S.op("vector", lambda e, dc=dc, pq=pq_lo: e.tensor_copy(
                        w2t_lo[:, dc * 128:(dc + 1) * 128], pq[:, 0:128]),
                        waits=[t_tr2], inc="v")
                t_w2t = (S.tok("ac"), S.tok("v"))
                for il in range(max(0, jl - 4), min(4, jl + 1)):
                    dj = jl - il
                    slot = il * 5 + dj
                    bank = 2 + (il % 2)
                    pA = pb[bank]
                    first = True
                    for dc in range(NCH):
                        l_hi = w1hi[:, il * D + dc * 128: il * D + (dc + 1) * 128]
                        l_lo = w1lo[:, il * D + dc * 128: il * D + (dc + 1) * 128]
                        r_hi = w2t_hi[:, dc * 128:(dc + 1) * 128]
                        r_lo = w2t_lo[:, dc * 128:(dc + 1) * 128]
                        wts = [t_w1split, t_w2t[0], t_w2t[1]] if first else []
                        if first:
                            wts = wts + [("v", S.val("v"))]
                        S.op("tensor", lambda e, a=l_hi, b=r_hi, pA=pA, f=first:
                             e.matmul(pA[:, 0:128], a, b, start=f, stop=False),
                             waits=wts, inc="pe")
                        S.op("tensor", lambda e, a=l_hi, b=r_lo, pA=pA:
                             e.matmul(pA[:, 0:128], a, b, start=False, stop=False),
                             inc="pe")
                        last = dc == NCH - 1
                        S.op("tensor", lambda e, a=l_lo, b=r_hi, pA=pA, l=last:
                             e.matmul(pA[:, 0:128], a, b, start=False, stop=l),
                             inc="pe")
                        first = False
                    t_mm = S.tok("pe")
                    S.op("vector", lambda e, slot=slot, pA=pA: e.tensor_mul(
                        amf[:], pA[:, 0:128], mkall[:, slot * 128:(slot + 1) * 128]),
                        waits=[t_mm, t_mk, ("ac", S.val("ac"))], inc="v")
                    t_amf = S.tok("v")
                    S.op("scalar", lambda e, slot=slot: e.copy(
                        a_hi[:, slot * 128:(slot + 1) * 128], amf[:]),
                        waits=[t_amf], inc="ac")
                    t_ahi = S.tok("ac")
                    S.op("vector", lambda e, slot=slot: e.tensor_sub(
                        a_lo[:, slot * 128:(slot + 1) * 128], amf[:],
                        a_hi[:, slot * 128:(slot + 1) * 128]),
                        waits=[t_ahi, ("v", S.val("v"))], inc="v")
            last_ac = S.tok("ac")
            last_v = S.tok("v")
            S.op("sync", lambda e: e.dma_start(
                out=cca_in[0].rearrange("p slot t -> p (slot t)"), in_=a_hi[:]),
                waits=[last_ac, last_v], inc="dca", inc_by=16)
            S.op("sync", lambda e: e.dma_start(
                out=cca_in[1].rearrange("p slot t -> p (slot t)"), in_=a_lo[:]),
                inc="dca", inc_by=16)
            t_dca = S.tok("dca")
            S.op("gpsimd", lambda e: e.collective_compute(
                "AllGather", OP.bypass,
                replica_groups=[list(range(8))],
                ins=[cca_in], outs=[cca_out]),
                waits=[t_dca], inc="cc")

            pkv = packeds.rearrange("(ns p) j -> ns p j", p=128)
            sclv = scaless.rearrange("(ns p) b -> ns p b", p=128)
            t_agA_emitted = S.tok("cc")
            for ns in range(4):
                for q in range(4):
                    S.op("sync", lambda e, ns=ns, q=q: e.dma_start(
                        out=pk[:], in_=pkv[ns, :, q * 512:(q + 1) * 512]),
                        waits=[("v", S.val("v"))], inc="dpk", inc_by=16)
                    if q == 0:
                        S.op("sync", lambda e, ns=ns: e.dma_start(
                            out=scl[:], in_=sclv[ns]), inc="dpk", inc_by=16)
                    t_pk = S.tok("dpk")
                    S.op("vector", lambda e: e.tensor_scalar(
                        nib[:, 0:512], pk[:], s15, None, OP.bitwise_and),
                        waits=[t_pk, t_const], inc="v")
                    S.op("vector", lambda e: e.tensor_scalar(
                        nib[:, 512:1024], pk[:], s240, None, OP.bitwise_and),
                        inc="v")
                    S.op("vector", lambda e: e.tensor_copy(wf[:, 0:512], nib[:, 0:512]),
                         waits=[("v", S.val("v"))], inc="v")
                    S.op("vector", lambda e: e.tensor_copy(wf[:, 512:1024], nib[:, 512:1024]),
                         inc="v")
                    S.op("vector", lambda e: e.tensor_single_scalar(
                        wf[:, 512:1024], wf[:, 512:1024], 0.0625, OP.mult),
                        waits=[("v", S.val("v"))], inc="v")
                    S.op("vector", lambda e, q=q: e.tensor_copy(
                        esc[:].rearrange("p (b j) -> p b j", b=32),
                        scl[:, 32 * q:32 * (q + 1)].unsqueeze(2).broadcast_to([128, 32, 16])),
                        inc="v")
                    w16v = w16[:].rearrange("p (j two) -> p j two", two=2)
                    S.op("vector", lambda e, w16v=w16v: e.scalar_tensor_tensor(
                        w16v[:, :, 0], wf[:, 0:512], 8.0, esc[:], OP.subtract, OP.mult),
                        waits=[("v", S.val("v"))], inc="v")
                    S.op("vector", lambda e, w16v=w16v: e.scalar_tensor_tensor(
                        w16v[:, :, 1], wf[:, 512:1024], 8.0, esc[:], OP.subtract, OP.mult),
                        inc="v")
                    S.op("vector", lambda e: e.tensor_copy(wde_hi[:], w16[:]),
                         waits=[("pe", S.val("pe")), ("v", S.val("v"))], inc="v")
                    S.op("vector", lambda e: e.tensor_sub(wde_lo[:], w16[:], wde_hi[:]),
                         waits=[("v", S.val("v"))], inc="v")
                    t_w = S.tok("v")
                    for kr in range(8):
                        kc = 8 * q + kr
                        for wsrc, wdst, eng, pq in (
                                (wde_hi, wt_hi, "scalar", pbb[0 + kr % 2]),
                                (wde_lo, wt_lo, "vector", pbb[2 + kr % 2])):
                            src = wsrc[:, kr * 128:(kr + 1) * 128]
                            guard = ("ac", S.val("ac")) if eng == "scalar" else ("v", S.val("v"))
                            S.op("tensor", lambda e, src=src, pq=pq:
                                 e.transpose(pq[:, 0:128], src, eyeb_s[:]),
                                 waits=[t_w, guard], inc="pe")
                            t_tr = S.tok("pe")
                            if eng == "scalar":
                                S.op(eng, lambda e, kc=kc, pq=pq, wdst=wdst:
                                     e.copy(wdst[:, kc * 128:(kc + 1) * 128], pq[:, 0:128]),
                                     waits=[t_tr], inc="ac")
                            else:
                                S.op(eng, lambda e, kc=kc, pq=pq, wdst=wdst:
                                     e.tensor_copy(wdst[:, kc * 128:(kc + 1) * 128],
                                                   pq[:, 0:128]),
                                     waits=[t_tr], inc="v")
                S.op("sync", lambda e, ns=ns: e.dma_start(
                    out=ccw_in[0, ns].rearrange("p kc n -> p (kc n)"), in_=wt_hi[:]),
                    waits=[S.tok("ac"), S.tok("v")], inc="dcw", inc_by=16)
                S.op("sync", lambda e, ns=ns: e.dma_start(
                    out=ccw_in[1, ns].rearrange("p kc n -> p (kc n)"), in_=wt_lo[:]),
                    inc="dcw", inc_by=16)
                t_dcw = S.tok("dcw")
                S.wait("scalar", [t_dcw])
                S.wait("vector", [t_dcw])
            S.op("gpsimd", lambda e: e.collective_compute(
                "AllGather", OP.bypass,
                replica_groups=[list(range(8))],
                ins=[ccw_in], outs=[ccw_out]),
                waits=[S.tok("dcw")], inc="cc")

            END_A = [("v", S.val("v")), ("ac", S.val("ac")), ("pe", S.val("pe")),
                     ("dca", S.val("dca")), ("dcw", S.val("dcw"))]
            for r in range(8):
                S.op("sync", lambda e, r=r: e.dma_start(
                    out=ares[:, r * NSLOT * 2 * 128:(r + 1) * NSLOT * 2 * 128]
                    .rearrange("p (slot half t) -> p slot half t", slot=NSLOT, half=2),
                    in_=cca_out[r].transpose([1, 2, 0, 3])),
                    waits=[("cc", 1)] + END_A, inc="dares", inc_by=16)
            t_ares = S.tok("dares")

            if DEBUG:
                S.op("gpsimd", lambda e: e.dma_start(out=dbg_a, in_=cca_out),
                     waits=[("cc", 1)], inc="dd", inc_by=16)
                S.op("gpsimd", lambda e: e.dma_start(out=dbg_w, in_=ccw_out[0]),
                     waits=[("cc", 2)], inc="dd", inc_by=16)

            _ALLSEMS = ("v", "ac", "pe", "dx", "dwh", "dwl", "dares", "dca",
                        "dcw", "din", "dy", "dd")

            def stage_barrier(S, name):
                if name not in BARRIERS:
                    return
                ws = [(s, S.val(s)) for s in _ALLSEMS]
                for eng in ("sync", "gpsimd", "vector", "scalar", "tensor"):
                    S.wait(eng, ws)

            def a_tile(I, J, half):
                r, slot = tile_slot(I, J)
                off = ((r * NSLOT + slot) * 2 + half) * 128
                return ares[:, off:off + 128]

            fwd_bank_free = [
                [("ac", S.val("ac")), ("v", S.val("v"))],
                [("ac", S.val("ac")), ("v", S.val("v"))],
            ]
            mm_bank_free = [[("ac", S.val("ac")), ("v", S.val("v"))],
                            [("ac", S.val("ac")), ("v", S.val("v"))]]
            bck_bank_free = [[("ac", S.val("ac")), ("v", S.val("v"))],
                             [("ac", S.val("ac")), ("v", S.val("v"))]]
            hch_free = [[("pe", S.val("pe")), ("dca", S.val("dca"))],
                        [("pe", S.val("pe")), ("dca", S.val("dca"))]]
            gts_free = [[("pe", S.val("pe"))], [("pe", S.val("pe"))]]
            wtb_free = [[("pe", S.val("pe"))], [("pe", S.val("pe"))]]

            def ln_stats(S, src_ap, su, t_src):
                for ch in range(8):
                    S.op("vector", lambda e, ch=ch: e.bn_stats(
                        stat[:, ch * 6:(ch + 1) * 6], src_ap[:, ch * 512:(ch + 1) * 512]),
                        waits=[t_src] if ch == 0 else [], inc="v")
                S.op("vector", lambda e: e.bn_aggr(stat[:, 48:50], stat[:, 0:48]),
                     waits=[("ac", S.val("ac")), ("v", S.val("v"))], inc="v")
                t_mv = S.tok("v")
                S.op("scalar", lambda e: e.activation(
                    stat[:, 50:51], stat[:, 49:50], AF.Sqrt, bias=eps_s[:]),
                    waits=[t_mv, t_const], inc="ac")
                t_sd = S.tok("ac")
                rstd = sc2[:, su * 2:su * 2 + 1]
                cc_ = sc2[:, su * 2 + 1:su * 2 + 2]
                S.op("vector", lambda e: e.tensor_single_scalar(
                    stat[:, 51:52], stat[:, 48:49], -1.0, OP.mult),
                    waits=[t_mv], inc="v")
                S.op("vector", lambda e, rstd=rstd: e.reciprocal(rstd, stat[:, 50:51]),
                     waits=[t_sd], inc="v")
                t_r = S.tok("v")
                S.op("vector", lambda e, rstd=rstd, cc_=cc_: e.tensor_mul(
                    cc_, stat[:, 51:52], rstd), waits=[t_r], inc="v")
                return S.tok("v"), rstd, cc_

            def transposes_to_hT(S, src, su, rstd, cc_, t_scal):
                for ch in range(NCH):
                    slot = ch % 2
                    S.op("scalar", lambda e, ch=ch, slot=slot: e.activation(
                        hch[:, slot * 128:(slot + 1) * 128],
                        src[:, ch * 128:(ch + 1) * 128],
                        AF.Identity, bias=cc_, scale=rstd),
                        waits=[t_scal] + hch_free[slot], inc="ac")
                    t_n = S.tok("ac")
                    bank = slot
                    S.op("tensor", lambda e, ch=ch, slot=slot, bank=bank: e.transpose(
                        pb[bank][:, 0:128], hch[:, slot * 128:(slot + 1) * 128], eyef_s[:]),
                        waits=[t_n] + fwd_bank_free[bank], inc="pe")
                    t_t = S.tok("pe")
                    hch_free[slot] = [t_t]
                    col = ch * PASS + su * 128
                    S.op("scalar", lambda e, col=col, bank=bank: e.copy(
                        hT_hi[:, col:col + 128], pb[bank][:, 0:128]),
                        waits=[t_t], inc="ac")
                    t_hi = S.tok("ac")
                    S.op("vector", lambda e, col=col, bank=bank: e.tensor_sub(
                        hT_lo[:, col:col + 128], pb[bank][:, 0:128],
                        hT_hi[:, col:col + 128]),
                        waits=[t_hi], inc="v")
                    fwd_bank_free[bank] = [S.tok("ac"), S.tok("v")]
                return S.tok("ac"), S.tok("v")

            def back_transpose(S, src_ap, dst, dst_col, t_src):
                for su in range(NSUB):
                    bank = 4 + su % 2
                    S.op("tensor", lambda e, su=su, bank=bank: e.transpose(
                        pb[bank][:, 0:128], src_ap[:, su * 128:(su + 1) * 128], eyef_s[:]),
                        waits=[t_src] + bck_bank_free[su % 2], inc="pe")
                    t_t = S.tok("pe")
                    S.op("vector", lambda e, su=su, bank=bank: e.tensor_copy(
                        dst[:, su * D + dst_col: su * D + dst_col + 128],
                        pb[bank][:, 0:128]),
                        waits=[t_t], inc="v")
                    bck_bank_free[su % 2] = [S.tok("v")]
                return S.tok("pe")

            for p in range(NPASS):
                stage_barrier(S, "s8")
                for su in range(NSUB):
                    r0 = p * PASS + su * 128
                    S.op("sync", lambda e, su=su, r0=r0: e.dma_start(
                        out=x0[:, su * D:(su + 1) * D], in_=xin[r0:r0 + 128, :]),
                        waits=([("v", S.val("v")), ("dy", S.val("dy"))] if p > 0
                               else [("cc", 1)]),
                        inc="dx", inc_by=16)
                t_x = S.tok("dx")
                stage_barrier(S, "s1")

                for su in range(NSUB):
                    t_scal, rstd, cc_ = ln_stats(S, x0[:, su * D:(su + 1) * D], su, t_x)
                    transposes_to_hT(S, x0[:, su * D:(su + 1) * D], su, rstd, cc_, t_scal)
                t_hT = (S.tok("ac"), S.tok("v"))
                stage_barrier(S, "s3")
                if DEBUG and p == 0:
                    S.op("sync", lambda e: e.dma_start(out=dbg_ht[0], in_=hT_hi[:]),
                         waits=[t_hT[0], t_hT[1]], inc="dd", inc_by=16)
                    S.op("sync", lambda e: e.dma_start(out=dbg_ht[1], in_=hT_lo[:]),
                         inc="dd", inc_by=16)

                for J in range(NCH):
                    bank = 2 + J % 2
                    pY = pb[bank]
                    first = True
                    for I in band_of(J):
                        ahi = a_tile(I, J, 0)
                        alo = a_tile(I, J, 1)
                        rhi = hT_hi[:, I * PASS:(I + 1) * PASS]
                        rlo = hT_lo[:, I * PASS:(I + 1) * PASS]
                        wts = ([t_ares, t_hT[0], t_hT[1]] + mm_bank_free[J % 2]) if first else []
                        S.op("tensor", lambda e, a=ahi, b=rhi, pY=pY, f=first:
                             e.matmul(pY[:, 0:PASS], a, b, start=f, stop=False),
                             waits=wts, inc="pe")
                        S.op("tensor", lambda e, a=ahi, b=rlo, pY=pY:
                             e.matmul(pY[:, 0:PASS], a, b, start=False, stop=False),
                             inc="pe")
                        last = I == band_of(J)[-1]
                        S.op("tensor", lambda e, a=alo, b=rhi, pY=pY, l=last:
                             e.matmul(pY[:, 0:PASS], a, b, start=False, stop=l),
                             inc="pe")
                        first = False
                    t_mm = S.tok("pe")
                    gslot = J % 2
                    S.op("scalar", lambda e, J=J, pY=pY, gslot=gslot: e.activation(
                        gts[:, gslot * PASS:(gslot + 1) * PASS], pY[:, 0:PASS],
                        AF.Gelu, bias=b2_s[:, J:J + 1], scale=1.0),
                        waits=[t_mm, t_const] + gts_free[gslot], inc="ac")
                    t_g = S.tok("ac")
                    mm_bank_free[J % 2] = [t_g]
                    t_bt = back_transpose(
                        S, gts[:, gslot * PASS:(gslot + 1) * PASS], gz, J * 128, t_g)
                    gts_free[gslot] = [t_bt]

                stage_barrier(S, "s4")
                if DEBUG and p == 0:
                    for su in range(NSUB):
                        S.op("sync", lambda e, su=su: e.dma_start(
                            out=dbg_g[su * 128:(su + 1) * 128, :],
                            in_=gz[:, su * D:(su + 1) * D]),
                            waits=[("v", S.val("v"))], inc="dd", inc_by=16)
                    S.wait("vector", [("dd", S.val("dd"))])
                    S.wait("scalar", [("dd", S.val("dd"))])

                for su in range(NSUB):
                    t_scal, rstd, cc_ = ln_stats(S, gz[:, su * D:(su + 1) * D], su,
                                                 ("v", S.val("v")))
                    S.op("scalar", lambda e, su=su, rstd=rstd, cc_=cc_: e.activation(
                        gz[:, su * D:(su + 1) * D], gz[:, su * D:(su + 1) * D],
                        AF.Identity, bias=cc_, scale=rstd),
                        waits=[t_scal], inc="ac")
                    t_n = S.tok("ac")
                    S.op("vector", lambda e, su=su: e.tensor_add(
                        x0[:, su * D:(su + 1) * D], x0[:, su * D:(su + 1) * D],
                        gz[:, su * D:(su + 1) * D]),
                        waits=[t_n], inc="v")
                t_x1 = S.tok("v")
                stage_barrier(S, "s5")
                if DEBUG:
                    for su in range(NSUB):
                        r0 = p * PASS + su * 128
                        S.op("sync", lambda e, su=su, r0=r0: e.dma_start(
                            out=dbg_x1[r0:r0 + 128, :], in_=x0[:, su * D:(su + 1) * D]),
                            waits=[t_x1], inc="dd", inc_by=16)
                    S.wait("vector", [("dd", S.val("dd"))])
                    S.wait("scalar", [("dd", S.val("dd"))])

                for su in range(NSUB):
                    t_scal, rstd, cc_ = ln_stats(S, x0[:, su * D:(su + 1) * D], su, t_x1)
                    transposes_to_hT(S, x0[:, su * D:(su + 1) * D], su, rstd, cc_, t_scal)
                t_h2T = (S.tok("ac"), S.tok("v"))
                stage_barrier(S, "s6")

                for nch in range(NCH):
                    rk, nsb = nch // 4, nch % 4
                    bank = 2 + nch % 2
                    pZ = pb[bank]
                    first = True
                    for ks in range(2):
                        buf = (2 * nch + ks) % 2
                        S.op("sync", lambda e, rk=rk, nsb=nsb, buf=buf, ks=ks: e.dma_start(
                            out=wtb_hi[buf][:].rearrange("p (kc n) -> p kc n", kc=16),
                            in_=ccw_out[rk, 0, nsb][:, ks * 16:(ks + 1) * 16, :]),
                            waits=[("cc", 2)] + wtb_free[buf], inc=f"dwh{buf}", inc_by=16)
                        S.op("gpsimd", lambda e, rk=rk, nsb=nsb, buf=buf, ks=ks: e.dma_start(
                            out=wtb_lo[buf][:].rearrange("p (kc n) -> p kc n", kc=16),
                            in_=ccw_out[rk, 1, nsb][:, ks * 16:(ks + 1) * 16, :]),
                            waits=[("cc", 2)] + wtb_free[buf], inc=f"dwl{buf}", inc_by=16)
                        t_wt = (S.tok(f"dwh{buf}"), S.tok(f"dwl{buf}"))
                        for kr in range(16):
                            kc = ks * 16 + kr
                            lhi = wtb_hi[buf][:, kr * 128:(kr + 1) * 128]
                            llo = wtb_lo[buf][:, kr * 128:(kr + 1) * 128]
                            rhi = hT_hi[:, kc * PASS:(kc + 1) * PASS]
                            rlo = hT_lo[:, kc * PASS:(kc + 1) * PASS]
                            wts = [t_wt[0], t_wt[1]]
                            if first:
                                wts = wts + [t_h2T[0], t_h2T[1]] + mm_bank_free[nch % 2]
                            elif kr > 0:
                                wts = []
                            S.op("tensor", lambda e, a=lhi, b=rhi, pZ=pZ, f=first:
                                 e.matmul(pZ[:, 0:PASS], a, b, start=f, stop=False),
                                 waits=wts, inc="pe")
                            S.op("tensor", lambda e, a=lhi, b=rlo, pZ=pZ:
                                 e.matmul(pZ[:, 0:PASS], a, b, start=False, stop=False),
                                 inc="pe")
                            last = kc == NCH - 1
                            S.op("tensor", lambda e, a=llo, b=rhi, pZ=pZ, l=last:
                                 e.matmul(pZ[:, 0:PASS], a, b, start=False, stop=l),
                                 inc="pe")
                            first = False
                        wtb_free[buf] = [S.tok("pe")]
                    t_mm = S.tok("pe")
                    gslot = nch % 2
                    S.op("scalar", lambda e, pZ=pZ, gslot=gslot: e.copy(
                        gts[:, gslot * PASS:(gslot + 1) * PASS], pZ[:, 0:PASS]),
                        waits=[t_mm] + gts_free[gslot], inc="ac")
                    t_z = S.tok("ac")
                    mm_bank_free[nch % 2] = [t_z]
                    t_bt = back_transpose(
                        S, gts[:, gslot * PASS:(gslot + 1) * PASS], gz, nch * 128, t_z)
                    gts_free[gslot] = [t_bt]

                stage_barrier(S, "s7")
                if DEBUG:
                    for su in range(NSUB):
                        r0 = p * PASS + su * 128
                        S.op("sync", lambda e, su=su, r0=r0: e.dma_start(
                            out=dbg_z[r0:r0 + 128, :], in_=gz[:, su * D:(su + 1) * D]),
                            waits=[("v", S.val("v"))], inc="dd", inc_by=16)
                    S.wait("vector", [("dd", S.val("dd"))])
                    S.wait("scalar", [("dd", S.val("dd"))])

                for su in range(NSUB):
                    t_scal, rstd, cc_ = ln_stats(S, gz[:, su * D:(su + 1) * D], su,
                                                 ("v", S.val("v")))
                    S.op("scalar", lambda e, su=su, rstd=rstd, cc_=cc_: e.activation(
                        gz[:, su * D:(su + 1) * D], gz[:, su * D:(su + 1) * D],
                        AF.Identity, bias=cc_, scale=rstd),
                        waits=[t_scal], inc="ac")
                    t_n = S.tok("ac")
                    S.op("vector", lambda e, su=su: e.tensor_add(
                        x0[:, su * D:(su + 1) * D], x0[:, su * D:(su + 1) * D],
                        gz[:, su * D:(su + 1) * D]),
                        waits=[t_n], inc="v")
                    t_out = S.tok("v")
                    r0 = p * PASS + su * 128
                    S.op("sync", lambda e, su=su, r0=r0: e.dma_start(
                        out=yout[r0:r0 + 128, :], in_=x0[:, su * D:(su + 1) * D]),
                        waits=[t_out], inc="dy", inc_by=16)

            S.wait("sync", [("dy", 16 * NSUB * NPASS)])
            S.wait("gpsimd", [("dy", 16 * NSUB * NPASS)])
            if DEBUG:
                S.wait("sync", [("dd", S.val("dd"))])
                S.wait("gpsimd", [("dd", S.val("dd"))])

        with nc.Block() as block:
            for eng_name, starter in (("sync", block.sync), ("vector", block.vector),
                                      ("scalar", block.scalar), ("tensor", block.tensor),
                                      ("gpsimd", block.gpsimd)):
                def make(eng_name):
                    def f(e):
                        program(Sched(eng_name, e, sems))
                    return f
                starter(make(eng_name))

    return nc


_NC_CACHE = None


def kernel(L0_attn_x, L0_attn_w1, L0_attn_w2, L0_attn_b2, L0_attn_sparse_mask,
           L0_ffn_weight_packed, L0_ffn_scales):
    global _NC_CACHE
    x = np.ascontiguousarray(np.asarray(L0_attn_x, dtype=np.float32).reshape(NCORES * TPC, D))
    w1 = np.asarray(L0_attn_w1, dtype=np.float32)
    w2 = np.asarray(L0_attn_w2, dtype=np.float32)
    b2 = np.asarray(L0_attn_b2, dtype=np.float32)
    mask = np.asarray(L0_attn_sparse_mask, dtype=np.float32)
    packed = np.asarray(L0_ffn_weight_packed, dtype=np.int32)
    scales = np.asarray(L0_ffn_scales, dtype=np.float16)

    eyef = np.eye(128, dtype=np.float32)
    eyeb = np.eye(128, dtype=ml_dtypes.bfloat16)
    ic2 = np.tile(np.array([[15, 240]], dtype=np.int32), (128, 1))
    b2t = np.ascontiguousarray(b2.reshape(32, 128).T)

    in_maps = []
    for c in range(NCORES):
        w2b = np.zeros((1024, D), dtype=np.float32)
        t0 = 512 * c - 256
        lo, hi = max(0, t0), min(D, t0 + 1024)
        w2b[lo - t0:hi - t0] = w2[lo:hi]
        mk = np.zeros((4, 5, 128, 128), dtype=np.float32)
        for il in range(4):
            I = 4 * c + il
            for dj in range(5):
                J = I + dj - 2
                if 0 <= J < NCH:
                    mk[il, dj] = mask[I * 128:(I + 1) * 128, J * 128:(J + 1) * 128]
        in_maps.append({
            "x": np.ascontiguousarray(x[c * TPC:(c + 1) * TPC]),
            "w1s": np.ascontiguousarray(w1[:, 512 * c:512 * (c + 1)]),
            "w2s": w2b,
            "masks": mk,
            "b2t": b2t,
            "packeds": np.ascontiguousarray(packed[512 * c:512 * (c + 1)]),
            "scaless": np.ascontiguousarray(scales[512 * c:512 * (c + 1)]),
            "eyef": eyef, "eyeb": eyeb, "ic2": ic2,
            "epsb": np.full((128, 1), EPS, dtype=np.float32),
        })

    if _NC_CACHE is None:
        _NC_CACHE = build_program()
    res = run_bass_kernel_spmd(_NC_CACHE, in_maps, core_ids=list(range(NCORES)))
    globals()["LAST_RESULTS"] = res.results
    out = np.concatenate([res.results[c]["y"] for c in range(NCORES)], axis=0)
    return out.reshape(2, 4096, 4096)


# version 08 (final)
# speedup vs baseline: 1.1458x; 1.1458x over previous
import math
import contextlib
import numpy as np
import ml_dtypes
import concourse.bass as bass
import concourse.mybir as mybir
from concourse.bass_utils import run_bass_kernel_spmd

F32 = mybir.dt.float32
BF16 = mybir.dt.bfloat16
F16 = mybir.dt.float16
I32 = mybir.dt.int32
AX = mybir.AxisListType.X
OP = mybir.AluOpType
AF = mybir.ActivationFunctionType

D = 4096
NCH = 32
NCORES = 8
TPC = 1024
PASS = 256
NPASS = TPC // PASS
NSUB = PASS // 128
EPS = 1e-5
NSLOT = 20
WINDOW, STRIPES = 6, 3
DEBUG = False
BARRIERS = set()


def _band_ok():
    stride = int(math.sqrt(D))
    nz = set()
    for k in range(STRIPES):
        ps = stride * k
        for i in range(0, D, WINDOW):
            r0, r1 = ps + i, min(ps + i + WINDOW, D)
            c0, c1 = i, min(i + WINDOW, D)
            if r0 >= D or c0 >= D or r1 <= r0 or c1 <= c0:
                continue
            for (a0, a1, b0, b1) in ((r0, r1, c0, c1), (c0, c1, r0, r1)):
                for I in range(a0 // 128, (a1 - 1) // 128 + 1):
                    for J in range(b0 // 128, (b1 - 1) // 128 + 1):
                        nz.add((I, J))
    assert all(abs(I - J) <= 2 for I, J in nz)
    return nz


_BAND = _band_ok()


def band_of(J):
    return [I for I in range(max(0, J - 2), min(NCH, J + 3))]


def tile_slot(I, J):
    return I // 4, (I % 4) * 5 + (J - I + 2)


class Sched:

    def __init__(self, emit_eng, eng_obj, sems):
        self.emit = emit_eng
        self.e = eng_obj
        self.sem = sems
        self.cnt = {k: 0 for k in sems}
        self.last_wait = {}

    def val(self, s):
        return self.cnt[s]

    def tok(self, s):
        return (s, self.cnt[s])

    def op(self, eng, fn, waits=(), inc=None, inc_by=1):
        if eng == self.emit:
            for s, v in waits:
                if v > 0 and self.last_wait.get(s, -1) < v:
                    self.e.wait_ge(self.sem[s], v)
                    self.last_wait[s] = v
            inst = fn(self.e)
            if inc is not None:
                inst.then_inc(self.sem[inc], inc_by)
        if inc is not None:
            self.cnt[inc] += inc_by

    def wait(self, eng, waits):
        if eng == self.emit:
            for s, v in waits:
                if v > 0 and self.last_wait.get(s, -1) < v:
                    self.e.wait_ge(self.sem[s], v)
                    self.last_wait[s] = v


def build_program():
    nc = bass.Bass("TRN2", target_bir_lowering=False, debug=False)

    xin = nc.dram_tensor("x", [TPC, D], F32, kind="ExternalInput").ap()
    w1s = nc.dram_tensor("w1s", [D, 512], F32, kind="ExternalInput").ap()
    w2s = nc.dram_tensor("w2s", [1024, D], F32, kind="ExternalInput").ap()
    masks = nc.dram_tensor("masks", [4, 5, 128, 128], F32, kind="ExternalInput").ap()
    b2t = nc.dram_tensor("b2t", [128, 32], F32, kind="ExternalInput").ap()
    packeds = nc.dram_tensor("packeds", [512, D // 2], I32, kind="ExternalInput").ap()
    scaless = nc.dram_tensor("scaless", [512, 128], F16, kind="ExternalInput").ap()
    eyef = nc.dram_tensor("eyef", [128, 128], F32, kind="ExternalInput").ap()
    eyeb = nc.dram_tensor("eyeb", [128, 128], BF16, kind="ExternalInput").ap()
    ic2 = nc.dram_tensor("ic2", [128, 2], I32, kind="ExternalInput").ap()
    epsb = nc.dram_tensor("epsb", [128, 1], F32, kind="ExternalInput").ap()
    yout = nc.dram_tensor("y", [TPC, D], F32, kind="ExternalOutput").ap()
    if DEBUG:
        dbg_x1 = nc.dram_tensor("dbg_x1", [TPC, D], F32, kind="ExternalOutput").ap()
        dbg_z = nc.dram_tensor("dbg_z", [TPC, D], F32, kind="ExternalOutput").ap()
        dbg_a = nc.dram_tensor("dbg_a", [8, 2, 128, NSLOT, 128], BF16, kind="ExternalOutput").ap()
        dbg_w = nc.dram_tensor("dbg_w", [2, 4, 128, NCH, 128], BF16, kind="ExternalOutput").ap()
        dbg_ht = nc.dram_tensor("dbg_ht", [2, 128, NCH * PASS], BF16, kind="ExternalOutput").ap()
        dbg_g = nc.dram_tensor("dbg_g", [PASS, D], F32, kind="ExternalOutput").ap()

    cca_in = nc.dram_tensor("cca_in", [2, 128, NSLOT, 128], BF16, kind="Internal").ap()
    cca_out = nc.dram_tensor("cca_out", [8, 2, 128, NSLOT, 128], BF16, kind="Internal").ap()
    ccw_in = nc.dram_tensor("ccw_in", [2, 4, 128, NCH, 128], BF16, kind="Internal").ap()
    ccw_out = nc.dram_tensor("ccw_out", [8, 2, 4, 128, NCH, 128], BF16, kind="Internal").ap()

    with contextlib.ExitStack() as stack:
        E = stack.enter_context

        eyef_s = E(nc.sbuf_tensor("eyef_s", [128, 128], F32))
        eyeb_s = E(nc.sbuf_tensor("eyeb_s", [128, 128], BF16))
        ic_s = E(nc.sbuf_tensor("ic_s", [128, 2], I32))
        b2_s = E(nc.sbuf_tensor("b2_s", [128, 32], F32))
        stat = E(nc.sbuf_tensor("stat", [128, 64], F32))
        sc2 = E(nc.sbuf_tensor("sc2", [128, 8], F32))
        eps_s = E(nc.sbuf_tensor("eps_s", [128, 1], F32))

        sems = {}
        for name in ("dc", "dw1", "dw2", "dpk", "dca", "dcw", "dares", "dx",
                     "dwh0", "dwh1", "dwh2", "dwl0", "dwl1", "dwl2", "dy",
                     "v", "ac", "pe", "cc", "dd"):
            sems[name] = E(nc.semaphore(name))

        stA = contextlib.ExitStack()
        EA = stA.enter_context
        pk = EA(nc.sbuf_tensor("pk", [128, 512], I32))
        nib = EA(nc.sbuf_tensor("nib", [128, 1024], I32))
        wf = EA(nc.sbuf_tensor("wf", [128, 1024], F32))
        scl = EA(nc.sbuf_tensor("scl", [128, 128], F16))
        esc = EA(nc.sbuf_tensor("esc", [128, 512], F32))
        w16 = EA(nc.sbuf_tensor("w16", [128, 1024], F16))
        wde_hi = EA(nc.sbuf_tensor("wde_hi", [128, 1024], BF16))
        wde_lo = EA(nc.sbuf_tensor("wde_lo", [128, 1024], BF16))
        wt_hi = EA(nc.sbuf_tensor("wt_hi", [128, NCH * 128], BF16))
        wt_lo = EA(nc.sbuf_tensor("wt_lo", [128, NCH * 128], BF16))
        w1stg = EA(nc.sbuf_tensor("w1stg", [128, D], F32))
        w1hi = EA(nc.sbuf_tensor("w1hi", [128, 4 * D], BF16))
        w1lo = EA(nc.sbuf_tensor("w1lo", [128, 4 * D], BF16))
        w2row = EA(nc.sbuf_tensor("w2row", [128, D], F32))
        w2hi = EA(nc.sbuf_tensor("w2hi", [128, D], BF16))
        w2lo = EA(nc.sbuf_tensor("w2lo", [128, D], BF16))
        w2t_hi = EA(nc.sbuf_tensor("w2t_hi", [128, D], BF16))
        w2t_lo = EA(nc.sbuf_tensor("w2t_lo", [128, D], BF16))
        mkall = EA(nc.sbuf_tensor("mkall", [128, NSLOT * 128], F32))
        amf = EA(nc.sbuf_tensor("amf", [128, 128], F32))
        a_hi = EA(nc.sbuf_tensor("a_hi", [128, NSLOT * 128], BF16))
        a_lo = EA(nc.sbuf_tensor("a_lo", [128, NSLOT * 128], BF16))

        pb = [E(nc.psum_tensor(f"pb{i}", [128, 512], F32)) for i in range(8)]
        pbb = [p[:].bitcast(BF16) for p in pb]

        stA.close()

        ares = E(nc.sbuf_tensor("ares", [128, 8 * NSLOT * 2 * 128], BF16))
        x0 = E(nc.sbuf_tensor("x0", [128, NSUB * D], F32))
        gz = E(nc.sbuf_tensor("gz", [128, NSUB * D], F32))
        hT_hi = E(nc.sbuf_tensor("hT_hi", [128, NCH * PASS], BF16))
        hT_lo = E(nc.sbuf_tensor("hT_lo", [128, NCH * PASS], BF16))
        wtb_hi = [E(nc.sbuf_tensor(f"wtb_hi{i}", [128, 16 * 128], BF16)) for i in range(3)]
        wtb_lo = [E(nc.sbuf_tensor(f"wtb_lo{i}", [128, 16 * 128], BF16)) for i in range(3)]
        hch = E(nc.sbuf_tensor("hch", [128, 8 * 128], F32))
        gts = E(nc.sbuf_tensor("gts", [128, 2 * PASS], F32))

        def program(S: Sched):
            for dst, srcap in ((eyef_s, eyef), (eyeb_s, eyeb), (ic_s, ic2),
                               (b2_s, b2t), (eps_s, epsb)):
                S.op("sync", lambda e, d=dst, s=srcap: e.dma_start(out=d[:], in_=s),
                     inc="dc", inc_by=16)
            S.op("sync", lambda e: e.dma_start(
                out=mkall[:].rearrange("p (il dj t) -> p il dj t", il=4, dj=5),
                in_=masks.transpose([2, 0, 1, 3])), inc="dc", inc_by=16)
            t_const = ("dc", 96)
            t_mk = ("dc", 96)
            S.wait("tensor", [t_const])

            s15 = ic_s[:, 0:1]
            s240 = ic_s[:, 1:2]

            pkv = packeds.rearrange("(ns p) j -> ns p j", p=128)
            sclv = scaless.rearrange("(ns p) b -> ns p b", p=128)
            for ns in range(4):
                for q in range(4):
                    S.op("sync", lambda e, ns=ns, q=q: e.dma_start(
                        out=pk[:], in_=pkv[ns, :, q * 512:(q + 1) * 512]),
                        waits=[("v", S.val("v"))], inc="dpk", inc_by=16)
                    if q == 0:
                        S.op("sync", lambda e, ns=ns: e.dma_start(
                            out=scl[:], in_=sclv[ns]), inc="dpk", inc_by=16)
                    t_pk = S.tok("dpk")
                    S.op("vector", lambda e: e.tensor_scalar(
                        nib[:, 0:512], pk[:], s15, None, OP.bitwise_and),
                        waits=[t_pk, t_const], inc="v")
                    S.op("vector", lambda e: e.tensor_scalar(
                        nib[:, 512:1024], pk[:], s240, None, OP.bitwise_and),
                        inc="v")
                    S.op("vector", lambda e: e.tensor_copy(wf[:, 0:512], nib[:, 0:512]),
                         waits=[("v", S.val("v"))], inc="v")
                    S.op("vector", lambda e: e.tensor_copy(wf[:, 512:1024], nib[:, 512:1024]),
                         inc="v")
                    S.op("vector", lambda e: e.tensor_single_scalar(
                        wf[:, 512:1024], wf[:, 512:1024], 0.0625, OP.mult),
                        waits=[("v", S.val("v"))], inc="v")
                    S.op("vector", lambda e, q=q: e.tensor_copy(
                        esc[:].rearrange("p (b j) -> p b j", b=32),
                        scl[:, 32 * q:32 * (q + 1)].unsqueeze(2).broadcast_to([128, 32, 16])),
                        inc="v")
                    w16v = w16[:].rearrange("p (j two) -> p j two", two=2)
                    S.op("vector", lambda e, w16v=w16v: e.scalar_tensor_tensor(
                        w16v[:, :, 0], wf[:, 0:512], 8.0, esc[:], OP.subtract, OP.mult),
                        waits=[("v", S.val("v"))], inc="v")
                    S.op("vector", lambda e, w16v=w16v: e.scalar_tensor_tensor(
                        w16v[:, :, 1], wf[:, 512:1024], 8.0, esc[:], OP.subtract, OP.mult),
                        inc="v")
                    S.op("vector", lambda e: e.tensor_copy(wde_hi[:], w16[:]),
                         waits=[("pe", S.val("pe")), ("v", S.val("v"))], inc="v")
                    S.op("vector", lambda e: e.tensor_sub(wde_lo[:], w16[:], wde_hi[:]),
                         waits=[("v", S.val("v"))], inc="v")
                    t_w = S.tok("v")
                    for kr in range(8):
                        kc = 8 * q + kr
                        for wsrc, wdst, eng, pq in (
                                (wde_hi, wt_hi, "scalar", pbb[0 + kr % 2]),
                                (wde_lo, wt_lo, "vector", pbb[2 + kr % 2])):
                            src = wsrc[:, kr * 128:(kr + 1) * 128]
                            guard = ("ac", S.val("ac")) if eng == "scalar" else ("v", S.val("v"))
                            S.op("tensor", lambda e, src=src, pq=pq:
                                 e.transpose(pq[:, 0:128], src, eyeb_s[:]),
                                 waits=[t_w, guard], inc="pe")
                            t_tr = S.tok("pe")
                            if eng == "scalar":
                                S.op(eng, lambda e, kc=kc, pq=pq, wdst=wdst:
                                     e.copy(wdst[:, kc * 128:(kc + 1) * 128], pq[:, 0:128]),
                                     waits=[t_tr], inc="ac")
                            else:
                                S.op(eng, lambda e, kc=kc, pq=pq, wdst=wdst:
                                     e.tensor_copy(wdst[:, kc * 128:(kc + 1) * 128],
                                                   pq[:, 0:128]),
                                     waits=[t_tr], inc="v")
                S.op("sync", lambda e, ns=ns: e.dma_start(
                    out=ccw_in[0, ns].rearrange("p kc n -> p (kc n)"), in_=wt_hi[:]),
                    waits=[S.tok("ac"), S.tok("v")], inc="dcw", inc_by=16)
                S.op("sync", lambda e, ns=ns: e.dma_start(
                    out=ccw_in[1, ns].rearrange("p kc n -> p (kc n)"), in_=wt_lo[:]),
                    inc="dcw", inc_by=16)
                t_dcw = S.tok("dcw")
                S.wait("scalar", [t_dcw])
                S.wait("vector", [t_dcw])
            S.op("gpsimd", lambda e: e.collective_compute(
                "AllGather", OP.bypass,
                replica_groups=[list(range(8))],
                ins=[ccw_in], outs=[ccw_out]),
                waits=[S.tok("dcw")], inc="cc")

            w1v = w1s.rearrange("(dc p) s -> dc p s", p=128)
            for il in range(4):
                S.op("sync", lambda e, il=il: e.dma_start(
                    out=w1stg[:].rearrange("p (dc s) -> p dc s", dc=NCH),
                    in_=w1v[:, :, il * 128:(il + 1) * 128].transpose([1, 0, 2])),
                    waits=[("v", S.val("v"))], inc="dw1", inc_by=16)
                t_w1 = S.tok("dw1")
                S.op("vector", lambda e, il=il: e.tensor_copy(
                    w1hi[:, il * D:(il + 1) * D], w1stg[:]),
                    waits=[t_w1], inc="v")
                S.op("vector", lambda e, il=il: e.tensor_sub(
                    w1lo[:, il * D:(il + 1) * D], w1stg[:],
                    w1hi[:, il * D:(il + 1) * D]),
                    waits=[("v", S.val("v"))], inc="v")
            t_w1split = S.tok("v")

            for jl in range(8):
                S.op("sync", lambda e, jl=jl: e.dma_start(
                    out=w2row[:], in_=w2s[jl * 128:(jl + 1) * 128, :]),
                    waits=[("v", S.val("v"))], inc="dw2", inc_by=16)
                t_w2r = S.tok("dw2")
                S.op("vector", lambda e: e.tensor_copy(w2hi[:], w2row[:]),
                     waits=[t_w2r, ("pe", S.val("pe"))], inc="v")
                S.op("vector", lambda e: e.tensor_sub(w2lo[:], w2row[:], w2hi[:]),
                     waits=[("v", S.val("v"))], inc="v")
                t_w2split = S.tok("v")
                for dc in range(NCH):
                    pq_hi = pbb[4 + dc % 2]
                    pq_lo = pbb[6 + dc % 2]
                    S.op("tensor", lambda e, dc=dc, pq=pq_hi: e.transpose(
                        pq[:, 0:128], w2hi[:, dc * 128:(dc + 1) * 128], eyeb_s[:]),
                        waits=[t_w2split, ("ac", S.val("ac"))], inc="pe")
                    t_tr1 = S.tok("pe")
                    S.op("tensor", lambda e, dc=dc, pq=pq_lo: e.transpose(
                        pq[:, 0:128], w2lo[:, dc * 128:(dc + 1) * 128], eyeb_s[:]),
                        waits=[("v", S.val("v"))], inc="pe")
                    t_tr2 = S.tok("pe")
                    S.op("scalar", lambda e, dc=dc, pq=pq_hi: e.copy(
                        w2t_hi[:, dc * 128:(dc + 1) * 128], pq[:, 0:128]),
                        waits=[t_tr1], inc="ac")
                    S.op("vector", lambda e, dc=dc, pq=pq_lo: e.tensor_copy(
                        w2t_lo[:, dc * 128:(dc + 1) * 128], pq[:, 0:128]),
                        waits=[t_tr2], inc="v")
                t_w2t = (S.tok("ac"), S.tok("v"))
                for il in range(max(0, jl - 4), min(4, jl + 1)):
                    dj = jl - il
                    slot = il * 5 + dj
                    bank = 2 + (il % 2)
                    pA = pb[bank]
                    first = True
                    for dc in range(NCH):
                        l_hi = w1hi[:, il * D + dc * 128: il * D + (dc + 1) * 128]
                        l_lo = w1lo[:, il * D + dc * 128: il * D + (dc + 1) * 128]
                        r_hi = w2t_hi[:, dc * 128:(dc + 1) * 128]
                        r_lo = w2t_lo[:, dc * 128:(dc + 1) * 128]
                        wts = [t_w1split, t_w2t[0], t_w2t[1]] if first else []
                        if first:
                            wts = wts + [("v", S.val("v"))]
                        S.op("tensor", lambda e, a=l_hi, b=r_hi, pA=pA, f=first:
                             e.matmul(pA[:, 0:128], a, b, start=f, stop=False),
                             waits=wts, inc="pe")
                        S.op("tensor", lambda e, a=l_hi, b=r_lo, pA=pA:
                             e.matmul(pA[:, 0:128], a, b, start=False, stop=False),
                             inc="pe")
                        last = dc == NCH - 1
                        S.op("tensor", lambda e, a=l_lo, b=r_hi, pA=pA, l=last:
                             e.matmul(pA[:, 0:128], a, b, start=False, stop=l),
                             inc="pe")
                        first = False
                    t_mm = S.tok("pe")
                    S.op("vector", lambda e, slot=slot, pA=pA: e.tensor_mul(
                        amf[:], pA[:, 0:128], mkall[:, slot * 128:(slot + 1) * 128]),
                        waits=[t_mm, t_mk, ("ac", S.val("ac"))], inc="v")
                    t_amf = S.tok("v")
                    S.op("scalar", lambda e, slot=slot: e.copy(
                        a_hi[:, slot * 128:(slot + 1) * 128], amf[:]),
                        waits=[t_amf], inc="ac")
                    t_ahi = S.tok("ac")
                    S.op("vector", lambda e, slot=slot: e.tensor_sub(
                        a_lo[:, slot * 128:(slot + 1) * 128], amf[:],
                        a_hi[:, slot * 128:(slot + 1) * 128]),
                        waits=[t_ahi, ("v", S.val("v"))], inc="v")
            last_ac = S.tok("ac")
            last_v = S.tok("v")
            S.op("sync", lambda e: e.dma_start(
                out=cca_in[0].rearrange("p slot t -> p (slot t)"), in_=a_hi[:]),
                waits=[last_ac, last_v], inc="dca", inc_by=16)
            S.op("sync", lambda e: e.dma_start(
                out=cca_in[1].rearrange("p slot t -> p (slot t)"), in_=a_lo[:]),
                inc="dca", inc_by=16)
            t_dca = S.tok("dca")
            S.op("gpsimd", lambda e: e.collective_compute(
                "AllGather", OP.bypass,
                replica_groups=[list(range(8))],
                ins=[cca_in], outs=[cca_out]),
                waits=[t_dca], inc="cc")

            END_A = [("v", S.val("v")), ("ac", S.val("ac")), ("pe", S.val("pe")),
                     ("dca", S.val("dca")), ("dcw", S.val("dcw"))]
            for r in range(8):
                S.op("sync", lambda e, r=r: e.dma_start(
                    out=ares[:, r * NSLOT * 2 * 128:(r + 1) * NSLOT * 2 * 128]
                    .rearrange("p (slot half t) -> p slot half t", slot=NSLOT, half=2),
                    in_=cca_out[r].transpose([1, 2, 0, 3])),
                    waits=[("cc", 2)] + END_A, inc="dares", inc_by=16)
            t_ares = S.tok("dares")

            if DEBUG:
                S.op("gpsimd", lambda e: e.dma_start(out=dbg_a, in_=cca_out),
                     waits=[("cc", 2)], inc="dd", inc_by=16)
                S.op("gpsimd", lambda e: e.dma_start(out=dbg_w, in_=ccw_out[0]),
                     waits=[("cc", 1)], inc="dd", inc_by=16)

            _ALLSEMS = ("v", "ac", "pe", "dx", "dwh", "dwl", "dares", "dca",
                        "dcw", "din", "dy", "dd")

            def stage_barrier(S, name):
                if name not in BARRIERS:
                    return
                ws = [(s, S.val(s)) for s in _ALLSEMS]
                for eng in ("sync", "gpsimd", "vector", "scalar", "tensor"):
                    S.wait(eng, ws)

            def a_tile(I, J, half):
                r, slot = tile_slot(I, J)
                off = ((r * NSLOT + slot) * 2 + half) * 128
                return ares[:, off:off + 128]

            fwd_bank_free = [
                [("ac", S.val("ac")), ("v", S.val("v"))],
                [("ac", S.val("ac")), ("v", S.val("v"))],
            ]
            MMBANKS = [2, 3, 6, 7]
            mm_bank_free = [[("ac", S.val("ac")), ("v", S.val("v"))]
                            for _ in range(4)]
            wt_tok = [None, None, None]
            bck_bank_free = [[("ac", S.val("ac")), ("v", S.val("v"))],
                             [("ac", S.val("ac")), ("v", S.val("v"))]]
            hch_free = [[("pe", S.val("pe")), ("dca", S.val("dca"))]
                        for _ in range(8)]
            gts_free = [[("pe", S.val("pe"))], [("pe", S.val("pe"))]]
            wtb_free = [[("pe", S.val("pe"))] for _ in range(3)]

            def ln_stats(S, src_ap, su, t_src):
                for ch in range(8):
                    S.op("vector", lambda e, ch=ch: e.bn_stats(
                        stat[:, ch * 6:(ch + 1) * 6], src_ap[:, ch * 512:(ch + 1) * 512]),
                        waits=[t_src] if ch == 0 else [], inc="v")
                S.op("vector", lambda e: e.bn_aggr(stat[:, 48:50], stat[:, 0:48]),
                     waits=[("ac", S.val("ac")), ("v", S.val("v"))], inc="v")
                t_mv = S.tok("v")
                S.op("scalar", lambda e: e.activation(
                    stat[:, 50:51], stat[:, 49:50], AF.Sqrt, bias=eps_s[:]),
                    waits=[t_mv, t_const], inc="ac")
                t_sd = S.tok("ac")
                rstd = sc2[:, su * 2:su * 2 + 1]
                cc_ = sc2[:, su * 2 + 1:su * 2 + 2]
                S.op("vector", lambda e: e.tensor_single_scalar(
                    stat[:, 51:52], stat[:, 48:49], -1.0, OP.mult),
                    waits=[t_mv], inc="v")
                S.op("vector", lambda e, rstd=rstd: e.reciprocal(rstd, stat[:, 50:51]),
                     waits=[t_sd], inc="v")
                t_r = S.tok("v")
                S.op("vector", lambda e, rstd=rstd, cc_=cc_: e.tensor_mul(
                    cc_, stat[:, 51:52], rstd), waits=[t_r], inc="v")
                return S.tok("v"), rstd, cc_

            def transposes_to_hT(S, src, su, rstd, cc_, t_scal):
                t_norm = [None] * NCH
                for k in range(8):
                    S.op("scalar", lambda e, k=k: e.activation(
                        hch[:, k * 128:(k + 1) * 128],
                        src[:, k * 128:(k + 1) * 128],
                        AF.Identity, bias=cc_, scale=rstd),
                        waits=[t_scal] + hch_free[k], inc="ac")
                    t_norm[k] = S.tok("ac")
                for ch in range(NCH):
                    slot = ch % 8
                    bank = ch % 2
                    S.op("tensor", lambda e, slot=slot, bank=bank: e.transpose(
                        pb[bank][:, 0:128], hch[:, slot * 128:(slot + 1) * 128],
                        eyef_s[:]),
                        waits=[t_norm[ch]] + fwd_bank_free[bank], inc="pe")
                    t_t = S.tok("pe")
                    hch_free[slot] = [t_t]
                    if ch + 8 < NCH:
                        k2 = ch + 8
                        S.op("scalar", lambda e, slot=slot, k2=k2: e.activation(
                            hch[:, slot * 128:(slot + 1) * 128],
                            src[:, k2 * 128:(k2 + 1) * 128],
                            AF.Identity, bias=cc_, scale=rstd),
                            waits=[t_t], inc="ac")
                        t_norm[k2] = S.tok("ac")
                    col = ch * PASS + su * 128
                    S.op("scalar", lambda e, col=col, bank=bank: e.copy(
                        hT_hi[:, col:col + 128], pb[bank][:, 0:128]),
                        waits=[t_t], inc="ac")
                    t_hi = S.tok("ac")
                    S.op("vector", lambda e, col=col, bank=bank: e.tensor_sub(
                        hT_lo[:, col:col + 128], pb[bank][:, 0:128],
                        hT_hi[:, col:col + 128]),
                        waits=[t_hi], inc="v")
                    fwd_bank_free[bank] = [S.tok("ac"), S.tok("v")]
                return S.tok("ac"), S.tok("v")

            def back_transpose(S, src_ap, dst, dst_col, t_src):
                for su in range(NSUB):
                    bank = 4 + su % 2
                    S.op("tensor", lambda e, su=su, bank=bank: e.transpose(
                        pb[bank][:, 0:128], src_ap[:, su * 128:(su + 1) * 128], eyef_s[:]),
                        waits=[t_src] + bck_bank_free[su % 2], inc="pe")
                    t_t = S.tok("pe")
                    S.op("vector", lambda e, su=su, bank=bank: e.tensor_copy(
                        dst[:, su * D + dst_col: su * D + dst_col + 128],
                        pb[bank][:, 0:128]),
                        waits=[t_t], inc="v")
                    bck_bank_free[su % 2] = [S.tok("v")]
                return S.tok("pe")

            for p in range(NPASS):
                stage_barrier(S, "s8")
                for su in range(NSUB):
                    r0 = p * PASS + su * 128
                    S.op("sync", lambda e, su=su, r0=r0: e.dma_start(
                        out=x0[:, su * D:(su + 1) * D], in_=xin[r0:r0 + 128, :]),
                        waits=([("v", S.val("v")), ("dy", S.val("dy"))] if p > 0
                               else [("cc", 2)]),
                        inc="dx", inc_by=16)
                t_x = S.tok("dx")
                stage_barrier(S, "s1")

                for su in range(NSUB):
                    t_scal, rstd, cc_ = ln_stats(S, x0[:, su * D:(su + 1) * D], su, t_x)
                    transposes_to_hT(S, x0[:, su * D:(su + 1) * D], su, rstd, cc_, t_scal)
                t_hT = (S.tok("ac"), S.tok("v"))
                stage_barrier(S, "s3")
                if DEBUG and p == 0:
                    S.op("sync", lambda e: e.dma_start(out=dbg_ht[0], in_=hT_hi[:]),
                         waits=[t_hT[0], t_hT[1]], inc="dd", inc_by=16)
                    S.op("sync", lambda e: e.dma_start(out=dbg_ht[1], in_=hT_lo[:]),
                         inc="dd", inc_by=16)

                pend = None
                for J in range(NCH):
                    bank = MMBANKS[J % 4]
                    pY = pb[bank]
                    first = True
                    for I in band_of(J):
                        ahi = a_tile(I, J, 0)
                        alo = a_tile(I, J, 1)
                        rhi = hT_hi[:, I * PASS:(I + 1) * PASS]
                        rlo = hT_lo[:, I * PASS:(I + 1) * PASS]
                        wts = ([t_ares, t_hT[0], t_hT[1]] + mm_bank_free[J % 4]) if first else []
                        S.op("tensor", lambda e, a=ahi, b=rhi, pY=pY, f=first:
                             e.matmul(pY[:, 0:PASS], a, b, start=f, stop=False),
                             waits=wts, inc="pe")
                        S.op("tensor", lambda e, a=ahi, b=rlo, pY=pY:
                             e.matmul(pY[:, 0:PASS], a, b, start=False, stop=False),
                             inc="pe")
                        last = I == band_of(J)[-1]
                        S.op("tensor", lambda e, a=alo, b=rhi, pY=pY, l=last:
                             e.matmul(pY[:, 0:PASS], a, b, start=False, stop=l),
                             inc="pe")
                        first = False
                    t_mm = S.tok("pe")
                    if pend is not None:
                        gs, Jp = pend
                        gts_free[gs] = [back_transpose(
                            S, gts[:, gs * PASS:(gs + 1) * PASS], gz, Jp * 128,
                            gpend_tok)]
                    gslot = J % 2
                    S.op("scalar", lambda e, J=J, pY=pY, gslot=gslot: e.activation(
                        gts[:, gslot * PASS:(gslot + 1) * PASS], pY[:, 0:PASS],
                        AF.Gelu, bias=b2_s[:, J:J + 1], scale=1.0),
                        waits=[t_mm, t_const] + gts_free[gslot], inc="ac")
                    gpend_tok = S.tok("ac")
                    mm_bank_free[J % 4] = [gpend_tok]
                    pend = (gslot, J)
                gs, Jp = pend
                gts_free[gs] = [back_transpose(
                    S, gts[:, gs * PASS:(gs + 1) * PASS], gz, Jp * 128, gpend_tok)]

                stage_barrier(S, "s4")
                if DEBUG and p == 0:
                    for su in range(NSUB):
                        S.op("sync", lambda e, su=su: e.dma_start(
                            out=dbg_g[su * 128:(su + 1) * 128, :],
                            in_=gz[:, su * D:(su + 1) * D]),
                            waits=[("v", S.val("v"))], inc="dd", inc_by=16)
                    S.wait("vector", [("dd", S.val("dd"))])
                    S.wait("scalar", [("dd", S.val("dd"))])

                for su in range(NSUB):
                    t_scal, rstd, cc_ = ln_stats(S, gz[:, su * D:(su + 1) * D], su,
                                                 ("v", S.val("v")))
                    S.op("scalar", lambda e, su=su, rstd=rstd, cc_=cc_: e.activation(
                        gz[:, su * D:(su + 1) * D], gz[:, su * D:(su + 1) * D],
                        AF.Identity, bias=cc_, scale=rstd),
                        waits=[t_scal], inc="ac")
                    t_n = S.tok("ac")
                    S.op("vector", lambda e, su=su: e.tensor_add(
                        x0[:, su * D:(su + 1) * D], x0[:, su * D:(su + 1) * D],
                        gz[:, su * D:(su + 1) * D]),
                        waits=[t_n], inc="v")
                t_x1 = S.tok("v")
                stage_barrier(S, "s5")
                if DEBUG:
                    for su in range(NSUB):
                        r0 = p * PASS + su * 128
                        S.op("sync", lambda e, su=su, r0=r0: e.dma_start(
                            out=dbg_x1[r0:r0 + 128, :], in_=x0[:, su * D:(su + 1) * D]),
                            waits=[t_x1], inc="dd", inc_by=16)
                    S.wait("vector", [("dd", S.val("dd"))])
                    S.wait("scalar", [("dd", S.val("dd"))])

                for su in range(NSUB):
                    t_scal, rstd, cc_ = ln_stats(S, x0[:, su * D:(su + 1) * D], su, t_x1)
                    transposes_to_hT(S, x0[:, su * D:(su + 1) * D], su, rstd, cc_, t_scal)
                t_h2T = (S.tok("ac"), S.tok("v"))
                stage_barrier(S, "s6")

                pend = None
                for nch in range(NCH):
                    rk, nsb = nch // 4, nch % 4
                    bank = MMBANKS[nch % 4]
                    pZ = pb[bank]
                    first = True
                    for ks in range(2):
                        buf = (2 * nch + ks) % 3
                        S.op("sync", lambda e, rk=rk, nsb=nsb, buf=buf, ks=ks: e.dma_start(
                            out=wtb_hi[buf][:].rearrange("p (kc n) -> p kc n", kc=16),
                            in_=ccw_out[rk, 0, nsb][:, ks * 16:(ks + 1) * 16, :]),
                            waits=[("cc", 1)] + wtb_free[buf], inc=f"dwh{buf}", inc_by=16)
                        S.op("gpsimd", lambda e, rk=rk, nsb=nsb, buf=buf, ks=ks: e.dma_start(
                            out=wtb_lo[buf][:].rearrange("p (kc n) -> p kc n", kc=16),
                            in_=ccw_out[rk, 1, nsb][:, ks * 16:(ks + 1) * 16, :]),
                            waits=[("cc", 1)] + wtb_free[buf], inc=f"dwl{buf}", inc_by=16)
                        wt_tok[buf] = [S.tok(f"dwh{buf}"), S.tok(f"dwl{buf}")]
                    for ks in range(2):
                        buf = (2 * nch + ks) % 3
                        t_wt = wt_tok[buf]
                        for kr in range(16):
                            kc = ks * 16 + kr
                            lhi = wtb_hi[buf][:, kr * 128:(kr + 1) * 128]
                            llo = wtb_lo[buf][:, kr * 128:(kr + 1) * 128]
                            rhi = hT_hi[:, kc * PASS:(kc + 1) * PASS]
                            rlo = hT_lo[:, kc * PASS:(kc + 1) * PASS]
                            wts = list(t_wt)
                            if first:
                                wts = wts + [t_h2T[0], t_h2T[1]] + mm_bank_free[nch % 4]
                            elif kr > 0:
                                wts = []
                            S.op("tensor", lambda e, a=lhi, b=rhi, pZ=pZ, f=first:
                                 e.matmul(pZ[:, 0:PASS], a, b, start=f, stop=False),
                                 waits=wts, inc="pe")
                            S.op("tensor", lambda e, a=lhi, b=rlo, pZ=pZ:
                                 e.matmul(pZ[:, 0:PASS], a, b, start=False, stop=False),
                                 inc="pe")
                            last = kc == NCH - 1
                            S.op("tensor", lambda e, a=llo, b=rhi, pZ=pZ, l=last:
                                 e.matmul(pZ[:, 0:PASS], a, b, start=False, stop=l),
                                 inc="pe")
                            first = False
                        wtb_free[buf] = [S.tok("pe")]
                    t_mm = S.tok("pe")
                    if pend is not None:
                        gs, np_ = pend
                        gts_free[gs] = [back_transpose(
                            S, gts[:, gs * PASS:(gs + 1) * PASS], gz, np_ * 128,
                            zpend_tok)]
                    gslot = nch % 2
                    S.op("scalar", lambda e, pZ=pZ, gslot=gslot: e.copy(
                        gts[:, gslot * PASS:(gslot + 1) * PASS], pZ[:, 0:PASS]),
                        waits=[t_mm] + gts_free[gslot], inc="ac")
                    zpend_tok = S.tok("ac")
                    mm_bank_free[nch % 4] = [zpend_tok]
                    pend = (gslot, nch)
                gs, np_ = pend
                gts_free[gs] = [back_transpose(
                    S, gts[:, gs * PASS:(gs + 1) * PASS], gz, np_ * 128, zpend_tok)]

                stage_barrier(S, "s7")
                if DEBUG:
                    for su in range(NSUB):
                        r0 = p * PASS + su * 128
                        S.op("sync", lambda e, su=su, r0=r0: e.dma_start(
                            out=dbg_z[r0:r0 + 128, :], in_=gz[:, su * D:(su + 1) * D]),
                            waits=[("v", S.val("v"))], inc="dd", inc_by=16)
                    S.wait("vector", [("dd", S.val("dd"))])
                    S.wait("scalar", [("dd", S.val("dd"))])

                for su in range(NSUB):
                    t_scal, rstd, cc_ = ln_stats(S, gz[:, su * D:(su + 1) * D], su,
                                                 ("v", S.val("v")))
                    S.op("scalar", lambda e, su=su, rstd=rstd, cc_=cc_: e.activation(
                        gz[:, su * D:(su + 1) * D], gz[:, su * D:(su + 1) * D],
                        AF.Identity, bias=cc_, scale=rstd),
                        waits=[t_scal], inc="ac")
                    t_n = S.tok("ac")
                    S.op("vector", lambda e, su=su: e.tensor_add(
                        x0[:, su * D:(su + 1) * D], x0[:, su * D:(su + 1) * D],
                        gz[:, su * D:(su + 1) * D]),
                        waits=[t_n], inc="v")
                    t_out = S.tok("v")
                    r0 = p * PASS + su * 128
                    S.op("sync", lambda e, su=su, r0=r0: e.dma_start(
                        out=yout[r0:r0 + 128, :], in_=x0[:, su * D:(su + 1) * D]),
                        waits=[t_out], inc="dy", inc_by=16)

            S.wait("sync", [("dy", 16 * NSUB * NPASS)])
            S.wait("gpsimd", [("dy", 16 * NSUB * NPASS)])
            if DEBUG:
                S.wait("sync", [("dd", S.val("dd"))])
                S.wait("gpsimd", [("dd", S.val("dd"))])

        with nc.Block() as block:
            for eng_name, starter in (("sync", block.sync), ("vector", block.vector),
                                      ("scalar", block.scalar), ("tensor", block.tensor),
                                      ("gpsimd", block.gpsimd)):
                def make(eng_name):
                    def f(e):
                        program(Sched(eng_name, e, sems))
                    return f
                starter(make(eng_name))

    return nc


_NC_CACHE = None


def kernel(L0_attn_x, L0_attn_w1, L0_attn_w2, L0_attn_b2, L0_attn_sparse_mask,
           L0_ffn_weight_packed, L0_ffn_scales):
    global _NC_CACHE
    x = np.ascontiguousarray(np.asarray(L0_attn_x, dtype=np.float32).reshape(NCORES * TPC, D))
    w1 = np.asarray(L0_attn_w1, dtype=np.float32)
    w2 = np.asarray(L0_attn_w2, dtype=np.float32)
    b2 = np.asarray(L0_attn_b2, dtype=np.float32)
    mask = np.asarray(L0_attn_sparse_mask, dtype=np.float32)
    packed = np.asarray(L0_ffn_weight_packed, dtype=np.int32)
    scales = np.asarray(L0_ffn_scales, dtype=np.float16)

    eyef = np.eye(128, dtype=np.float32)
    eyeb = np.eye(128, dtype=ml_dtypes.bfloat16)
    ic2 = np.tile(np.array([[15, 240]], dtype=np.int32), (128, 1))
    b2t = np.ascontiguousarray(b2.reshape(32, 128).T)

    in_maps = []
    for c in range(NCORES):
        w2b = np.zeros((1024, D), dtype=np.float32)
        t0 = 512 * c - 256
        lo, hi = max(0, t0), min(D, t0 + 1024)
        w2b[lo - t0:hi - t0] = w2[lo:hi]
        mk = np.zeros((4, 5, 128, 128), dtype=np.float32)
        for il in range(4):
            I = 4 * c + il
            for dj in range(5):
                J = I + dj - 2
                if 0 <= J < NCH:
                    mk[il, dj] = mask[I * 128:(I + 1) * 128, J * 128:(J + 1) * 128]
        in_maps.append({
            "x": np.ascontiguousarray(x[c * TPC:(c + 1) * TPC]),
            "w1s": np.ascontiguousarray(w1[:, 512 * c:512 * (c + 1)]),
            "w2s": w2b,
            "masks": mk,
            "b2t": b2t,
            "packeds": np.ascontiguousarray(packed[512 * c:512 * (c + 1)]),
            "scaless": np.ascontiguousarray(scales[512 * c:512 * (c + 1)]),
            "eyef": eyef, "eyeb": eyeb, "ic2": ic2,
            "epsb": np.full((128, 1), EPS, dtype=np.float32),
        })

    if _NC_CACHE is None:
        _NC_CACHE = build_program()
    res = run_bass_kernel_spmd(_NC_CACHE, in_maps, core_ids=list(range(NCORES)))
    globals()["LAST_RESULTS"] = res.results
    out = np.concatenate([res.results[c]["y"] for c in range(NCORES)], axis=0)
    return out.reshape(2, 4096, 4096)
